# version 09 (final)
# speedup vs baseline: 1.8756x; 1.0250x over previous
import numpy as np
import concourse.bass as bass
import concourse.mybir as mybir
from concourse.bass_utils import run_bass_kernel_spmd

F32 = mybir.dt.float32
BF16 = mybir.dt.bfloat16
I32 = mybir.dt.int32
AX = mybir.AxisListType.X
ALU = mybir.AluOpType
ACT = mybir.ActivationFunctionType

NB = 4
NT = 2048
C = 1024
KC = 8
HC = 24
NSL = 4
SL = 512
EPS = 1e-5
QMAX = 32767.0

DEBUG_TAPS = []
DEBUG_DUMPS = []


class Sched:

    ENGINES = ("tensor", "vector", "scalar", "sync", "gpsimd")
    DMA_ENGINES = ("sync", "gpsimd")
    NCH = 10

    def __init__(self):
        self.ops = []
        self.cnt = {}
        for e in self.ENGINES:
            nch = self.NCH if e in self.DMA_ENGINES else 1
            for ch in range(nch):
                self.cnt[(e, ch)] = 0
        self.cnt[("gpsimd", "cc")] = 0
        self.rr = {e: 0 for e in self.DMA_ENGINES}
        self.last_w = {}
        self.readers = {}

    def add(self, eng, fn, reads=(), writes=(), dma=False, inc=None, cc=False):
        if inc is None:
            inc = 16 if dma else 1
        if cc:
            ch = "cc"
        elif eng in self.DMA_ENGINES:
            assert dma, f"non-DMA op on {eng}"
            ch = self.rr[eng]
            self.rr[eng] = (ch + 1) % self.NCH
        else:
            assert not dma
            ch = 0
        me_ec = (eng, ch)
        deps = {}

        def dep(ec, c2):
            if ec == me_ec and not dma and eng == "tensor":
                return
            if deps.get(ec, 0) < c2:
                deps[ec] = c2

        if dma:
            prev = self.cnt[me_ec]
            if prev > 0:
                deps[me_ec] = prev
        for k in reads:
            lw = self.last_w.get(k)
            if lw:
                dep(*lw)
        for k in writes:
            lw = self.last_w.get(k)
            if lw:
                dep(*lw)
            for r in self.readers.get(k, ()):
                dep(*r)
        self.cnt[me_ec] += inc
        me = (me_ec, self.cnt[me_ec])
        for k in writes:
            self.last_w[k] = me
            self.readers[k] = []
        for k in reads:
            self.readers.setdefault(k, []).append(me)
        self.ops.append((eng, fn, inc, deps, ch))

    def emit(self, nc, block, sems):
        per = {e: [] for e in self.ENGINES}
        for eng, fn, inc, deps, ch in self.ops:
            per[eng].append((fn, inc, deps, ch))
        final = dict(self.cnt)

        def run_engine(e, name):
            watermark = {ec: 0 for ec in self.cnt}
            for fn, inc, deps, ch in per[name]:
                for dec, dc in sorted(deps.items(), key=lambda x: (str(x[0][0]), str(x[0][1]))):
                    if watermark[dec] < dc:
                        e.wait_ge(sems[dec], dc)
                        watermark[dec] = dc
                fn(e).then_inc(sems[(name, ch)], inc)
            if name == "sync":
                for dec in sorted(self.cnt, key=lambda x: (str(x[0]), str(x[1]))):
                    if final[dec] > 0:
                        e.wait_ge(sems[dec], final[dec])

        @block.tensor
        def _(e):
            run_engine(e, "tensor")

        @block.vector
        def _(e):
            run_engine(e, "vector")

        @block.scalar
        def _(e):
            run_engine(e, "scalar")

        @block.sync
        def _(e):
            run_engine(e, "sync")

        @block.gpsimd
        def _(e):
            run_engine(e, "gpsimd")


class Tile:
    __slots__ = ("ap", "key")

    def __init__(self, ap, key):
        self.ap = ap
        self.key = key

    def __getitem__(self, idx):
        return Tile(self.ap[idx], self.key)
def build_program():
    nc = bass.Bass("TRN2", target_bir_lowering=False, debug=False)

    def din(name, shape, dt=F32):
        return nc.dram_tensor(name, shape, dt, kind="ExternalInput").ap()

    x0 = din("x0", [NB, C, NT])
    wqkv = din("wqkv", [C, 3 * C])
    wproj = din("wproj", [C, C])
    wg_n = din("wg", [3 * C, C])
    wu_n = din("wu", [3 * C, C])
    wd_n = din("wd", [C, 3 * C])
    eye = din("eye", [128, 128])
    d7 = din("d7", [128, 128])
    d15 = din("d15", [128, 128])
    d325 = din("d325", [128, 128])
    bones = din("bones", [128, 128])
    onesb = din("onesb", [128, 128], BF16)

    yout = nc.dram_tensor("y", [NB, C, NT], F32, kind="ExternalOutput").ap()

    def scr(name, shape, dt=F32):
        return nc.dram_tensor(name, shape, dt).ap()

    s_wq_h, s_wq_l = scr("s_wq_h", [C, 3 * C], BF16), scr("s_wq_l", [C, 3 * C], BF16)
    s_wp_h, s_wp_l = scr("s_wp_h", [C, C], BF16), scr("s_wp_l", [C, C], BF16)
    s_wg_h, s_wg_l = scr("s_wg_h", [C, 3 * C], BF16), scr("s_wg_l", [C, 3 * C], BF16)
    s_wu_h, s_wu_l = scr("s_wu_h", [C, 3 * C], BF16), scr("s_wu_l", [C, 3 * C], BF16)
    s_wd_h, s_wd_l = scr("s_wd_h", [3 * C, C], BF16), scr("s_wd_l", [3 * C, C], BF16)
    s_q = scr("s_q", [NB, C, NT])
    s_k = scr("s_k", [NB, C, NT])
    s_vt = scr("s_vt", [NB, C, NT])
    s_x1 = scr("s_x1", [NB, C, NT])
    s_ln3 = scr("s_ln3", [NB, C, NT])
    s_Rh = scr("s_Rh", [C, NB * NT], BF16)
    s_Rl = scr("s_Rl", [C, NB * NT], BF16)
    s_gate = scr("s_gate", [3 * C, NB * NT])
    s_up = scr("s_up", [3 * C, NB * NT])
    s_g2r = scr("s_g2r", [3 * C, NB * NT])
    s_or = scr("s_or", [3 * C, NB * NT])
    s_roh = scr("s_roh", [3 * C, NB * NT], BF16)
    s_rol = scr("s_rol", [3 * C, NB * NT], BF16)
    s_sc = scr("s_sc", [1, 16])
    cc_in = [scr(f"cc_in{i}", [1, 1]) for i in range(5)]
    cc_out = [scr(f"cc_out{i}", [1, 1]) for i in range(5)]

    taps = {name: nc.dram_tensor(f"tap_{name}", shape, F32, kind="ExternalOutput").ap()
            for name, shape in DEBUG_TAPS}

    sch = Sched()
    from contextlib import ExitStack
    stack = ExitStack()

    def sb(name, shape, dt=F32):
        h = stack.enter_context(nc.sbuf_tensor(name, shape, dt))
        return Tile(h[:], name)

    def ps(name, shape):
        h = stack.enter_context(nc.psum_tensor(name, shape, F32))
        return Tile(h[:], name)

    with stack:
        pool = [sb(f"pool{i}", [128, 16384], BF16) for i in range(4)]
        poolf = [Tile(p.ap.bitcast(F32), p.key) for p in pool]

        def pv(i, c0, w=2048):
            return Tile(poolf[i].ap[:, c0:c0 + w], pool[i].key)

        rot = [sb(f"rot{i}", [128, 2048]) for i in range(4)]
        rbh = [sb(f"rbh{i}", [128, 2048], BF16) for i in range(2)]
        rbl = [sb(f"rbl{i}", [128, 2048], BF16) for i in range(2)]
        rint = sb("rint", [128, 2048], I32)

        nm128 = ["blq", "blk", "kbd", "ktb", "vv", "vt", "kv", "kvt",
                 "ta", "k3v", "m2", "tr1", "sI", "sD7", "sD15", "sD325", "sBON"]
        s128 = {n: sb("t128_" + n, [128, 128]) for n in nm128}
        sONEb = sb("s_onesb", [128, 128], BF16)
        lmq = sb("lmq", [128, 64])
        lmk = sb("lmk", [128, 64])

        w1h2 = [sb(f"w1h{i}", [128, 1024], BF16) for i in range(2)]
        w1l2 = [sb(f"w1l{i}", [128, 1024], BF16) for i in range(2)]

        ab_a = sb("ab_a", [128, 512])
        ab_b = sb("ab_b", [128, 512])
        robh = [sb(f"robh{i}", [128, 256], BF16) for i in range(2)]
        robl = [sb(f"robl{i}", [128, 256], BF16) for i in range(2)]
        sm = sb("sm", [128, 16])
        svec = sb("svec", [1, 16])
        scal = sb("scal", [128, 16])

        psbig = ps("psbig", [128, 2048])
        p512 = [ps(f"p512_{i}", [128, 512]) for i in range(4)]


        sems = {}
        from contextlib import ExitStack as ES2
        semstack = ES2()
        for e in Sched.ENGINES:
            nch = Sched.NCH if e in Sched.DMA_ENGINES else 1
            for ch in range(nch):
                sems[(e, ch)] = semstack.enter_context(
                    nc.semaphore(f"sem_{e}_{ch}"))
        sems[("gpsimd", "cc")] = semstack.enter_context(nc.semaphore("sem_cc"))

        with semstack, nc.Block() as block:
            A = sch.add

            def dma(eng, dst, src, reads=(), writes=()):
                A(eng, lambda e, d=dst, s=src: e.dma_start(out=d.ap, in_=s.ap),
                  reads=list(reads) + [src.key], writes=list(writes) + [dst.key],
                  dma=True)

            def dma_raw(eng, dst_ap, src_ap, reads=(), writes=()):
                A(eng, lambda e, d=dst_ap, s=src_ap: e.dma_start(out=d, in_=s),
                  reads=reads, writes=writes, dma=True)

            def v(fn, reads=(), writes=()):
                A("vector", fn, reads=reads, writes=writes)

            def sc_(fn, reads=(), writes=()):
                A("scalar", fn, reads=reads, writes=writes)

            def mm(out, lh, rh, start, stop, reads=(), writes=()):
                A("tensor",
                  lambda e, o=out, a=lh, b=rh, s1=start, s2=stop:
                      e.matmul(o.ap, a.ap, b.ap, start=s1, stop=s2),
                  reads=[lh.key, rh.key] + list(reads),
                  writes=[out.key] + list(writes))

            def mm3(out, lh, ll, rh, rl, start, stop):
                mm(out, lh, rh, start, False)
                mm(out, lh, rl, False, False)
                mm(out, ll, rh, False, stop)

            def transpose(out_ps, src):
                A("tensor",
                  lambda e, o=out_ps, s=src: e.transpose(o.ap, s.ap, s128["sI"].ap),
                  reads=[src.key, "t128_sI"],
                  writes=[out_ps.key])

            def split(dst_h, dst_l, src, eng_h="scalar"):
                if eng_h == "scalar":
                    sc_(lambda e, d=dst_h, s=src: e.copy(d.ap, s.ap),
                        reads=[src.key], writes=[dst_h.key])
                else:
                    v(lambda e, d=dst_h, s=src: e.tensor_copy(d.ap, s.ap),
                      reads=[src.key], writes=[dst_h.key])
                v(lambda e, d=dst_l, s=src, h=dst_h:
                      e.tensor_tensor(d.ap, s.ap, h.ap, ALU.subtract),
                  reads=[src.key, dst_h.key], writes=[dst_l.key])

            def _round_core(dst, z, rf, dtile, ri, rikey):
                v(lambda e, a=ri, b=z: e.tensor_copy(a.ap, b.ap),
                  reads=[z.key], writes=[rikey])
                v(lambda e, a=rf, b=ri: e.tensor_scalar(
                      a.ap, b.ap, -1.0, None, ALU.add),
                  reads=[rikey], writes=[rf.key])
                v(lambda e, a=dtile, b=z, c=rf: e.scalar_tensor_tensor(
                      a.ap, b.ap, -1.0, c.ap, ALU.add, ALU.subtract),
                  reads=[z.key, rf.key], writes=[dtile.key])
                v(lambda e, a=dst, b=dtile, c=rf:
                      e.scalar_tensor_tensor(a.ap, b.ap, 0.5, c.ap,
                                             ALU.is_gt, ALU.add),
                  reads=[dtile.key, rf.key], writes=[dst.key])
                v(lambda e, a=dst, b=dtile:
                      e.scalar_tensor_tensor(a.ap, b.ap, -0.5, a.ap,
                                             ALU.is_ge, ALU.add),
                  reads=[dtile.key, dst.key], writes=[dst.key])

            def round_ip2(dst, z, rf, dtile, pp):
                W = z.ap.shape[1]
                ri = Tile(rint.ap[:, pp * W:(pp + 1) * W], f"rint.{pp}")
                _round_core(dst, z, rf, dtile, ri, f"rint.{pp}")

            def round_ip(dst, z, rf, dtile):
                W = z.ap.shape[1]
                ri = Tile(rint.ap[:, 0:W], rint.key)
                _round_core(dst, z, rf, dtile, ri, rint.key)

            def bridge(keys_a, keys_b):
                v(lambda e: e.memset(sm.ap[0:1, 15:16], 0.0),
                  reads=[], writes=list(keys_a) + list(keys_b))

            def pmax_scalar(col_tile, dram_idx, out_col):
                pt = Tile(p512[3].ap[0:1, 0:128], p512[3].key)
                transpose(pt, col_tile)
                v(lambda e, o=svec, i=pt, di=dram_idx:
                      e.reduce_max(o.ap[0:1, di:di + 1], i.ap, axis=AX),
                  reads=[p512[3].key], writes=[svec.key])
                dma_raw("gpsimd", s_sc[0:1, dram_idx:dram_idx + 1],
                        svec.ap[0:1, dram_idx:dram_idx + 1],
                        reads=[svec.key], writes=["s_sc"])
                dma_raw("gpsimd", scal.ap[:, out_col:out_col + 1],
                        s_sc[0:1, dram_idx:dram_idx + 1].partition_broadcast(128),
                        reads=["s_sc"], writes=[f"sc{out_col}"])

            def col(i):
                return Tile(scal.ap[:, i:i + 1], f"sc{i}")

            def allreduce_max(idx, src_col, dst_col):
                dma_raw("gpsimd", cc_in[idx][0:1, 0:1],
                        scal.ap[0:1, src_col:src_col + 1],
                        reads=[f"sc{src_col}"], writes=[f"cc_in{idx}"])
                A("gpsimd",
                  lambda e, i=idx: e.collective_compute(
                      "AllReduce", ALU.max, replica_groups=[list(range(8))],
                      ins=[cc_in[i]], outs=[cc_out[i]]),
                  reads=[f"cc_in{idx}"], writes=[f"cc_out{idx}"], dma=True,
                  inc=1, cc=True)
                dma_raw("gpsimd", scal.ap[:, dst_col:dst_col + 1],
                        cc_out[idx][0:1, 0:1].partition_broadcast(128),
                        reads=[f"cc_out{idx}"], writes=[f"sc{dst_col}"])

            def tap(name, src, rows=128):
                if name in taps:
                    dma_raw("gpsimd", taps[name][0:rows, :], src.ap,
                            reads=[src.key], writes=[])

            v(lambda e: e.memset(scal.ap, 0.0), reads=[],
              writes=[f"sc{i}" for i in range(16)])
            v(lambda e: e.memset(scal.ap[:, 14:15], float(EPS)),
              reads=["sc14"], writes=["sc14"])
            v(lambda e: e.memset(sm.ap, 0.0), reads=[], writes=[sm.key])
            v(lambda e: e.memset(svec.ap, 0.0), reads=[], writes=[svec.key])

            dma("sync", s128["sI"], Tile(eye, "eye"))
            dma("sync", s128["sD7"], Tile(d7, "d7"))
            dma("sync", s128["sD15"], Tile(d15, "d15"))
            dma("sync", s128["sD325"], Tile(d325, "d325"))
            dma("sync", s128["sBON"], Tile(bones, "bones"))
            dma("sync", sONEb, Tile(onesb, "onesb"))

            wjobs = [(wqkv, s_wq_h, s_wq_l, C, 3 * C),
                     (wproj, s_wp_h, s_wp_l, C, C)]
            for wsrc, wh_, wl_, R, Cc in wjobs:
                for r in range(R // 128):
                    for cb in range(Cc // 1024):
                        c0 = cb * 1024
                        wt0 = Tile(rot[0].ap[:, 0:1024], rot[0].key)
                        wth = Tile(rbh[0].ap[:, 0:1024], rbh[0].key)
                        wtl = Tile(rbl[0].ap[:, 0:1024], rbl[0].key)
                        dma_raw("sync", wt0.ap,
                                wsrc[r * 128:(r + 1) * 128, c0:c0 + 1024],
                                reads=[], writes=[wt0.key])
                        split(wth, wtl, wt0)
                        dma_raw("gpsimd", wh_[r * 128:(r + 1) * 128, c0:c0 + 1024],
                                wth.ap, reads=[wth.key], writes=["whbm"])
                        dma_raw("gpsimd", wl_[r * 128:(r + 1) * 128, c0:c0 + 1024],
                                wtl.ap, reads=[wtl.key], writes=["wlbm"])

            lpjobs = [(wg_n, s_wg_h, s_wg_l, 3 * C, C),
                      (wu_n, s_wu_h, s_wu_l, 3 * C, C),
                      (wd_n, s_wd_h, s_wd_l, C, 3 * C)]
            bridge(['rot0', 'rot1', 'rot2', 'rot3', 'rbh0', 'rbl0', 'lmq', 'lmk', 'rint'], ['rot0.0', 'rot0.1', 'rot1.0', 'rot1.1', 'rot2.0', 'rot2.1', 'rot3.0', 'rot3.1', 'rbh0.0', 'rbh0.1', 'rbl0.0', 'rbl0.1', 'lmq.0', 'lmq.1', 'lmk.0', 'lmk.1', 'rint.0', 'rint.1'])
            lp_idx = [0]
            for wsrc, wh_, wl_, R, Cc in lpjobs:
                for r in range(R // 128):
                    for cb in range(Cc // 1024):
                        c0 = cb * 1024
                        pp = lp_idx[0] % 2
                        lp_idx[0] += 1
                        o0 = pp * 1024
                        wt0 = Tile(rot[0].ap[:, o0:o0 + 1024], rot[0].key)
                        dma_raw("sync", wt0.ap,
                                wsrc[r * 128:(r + 1) * 128, c0:c0 + 1024],
                                reads=[], writes=[wt0.key + f".{pp}"])
                        sab = Tile(lmq.ap[:, pp * 32:pp * 32 + 32], f"lmq.{pp}")
                        v(lambda e, o=sab, i=wt0: e.reduce_max(
                              o.ap, i.ap.rearrange("p (g w) -> p g w", w=32),
                              axis=AX, apply_absolute_value=True),
                          reads=[wt0.key + f".{pp}"], writes=[sab.key])
                        v(lambda e, o=sab: e.tensor_scalar(
                              o.ap, o.ap, 1.0 / 7.0, 1e-12, ALU.mult, ALU.max),
                          reads=[sab.key], writes=[sab.key])
                        rsb = Tile(lmk.ap[:, pp * 32:pp * 32 + 32], f"lmk.{pp}")
                        v(lambda e, o=rsb, i=sab: e.reciprocal(o.ap, i.ap),
                          reads=[sab.key], writes=[rsb.key])
                        zt = Tile(rot[1].ap[:, o0:o0 + 1024], f"rot1.{pp}")
                        v(lambda e, o=zt, i=wt0, s_=rsb: e.tensor_tensor(
                              o.ap.rearrange("p (g w) -> p g w", w=32),
                              i.ap.rearrange("p (g w) -> p g w", w=32),
                              s_.ap.unsqueeze(2).broadcast_to([128, 32, 32]),
                              ALU.mult),
                          reads=[wt0.key + f".{pp}", rsb.key], writes=[zt.key])
                        Rt = Tile(rot[2].ap[:, o0:o0 + 1024], f"rot2.{pp}")
                        rft = Tile(rot[3].ap[:, o0:o0 + 1024], f"rot3.{pp}")
                        round_ip2(Rt, zt, rft, zt, pp)
                        wqt = Tile(rot[1].ap[:, o0:o0 + 1024], zt.key)
                        v(lambda e, o=wqt, i=Rt, s_=sab: e.tensor_tensor(
                              o.ap.rearrange("p (g w) -> p g w", w=32),
                              i.ap.rearrange("p (g w) -> p g w", w=32),
                              s_.ap.unsqueeze(2).broadcast_to([128, 32, 32]),
                              ALU.mult),
                          reads=[Rt.key, sab.key], writes=[wqt.key])
                        for cs in range(8):
                            ptw = Tile(p512[pp].ap[:, 0:128], p512[pp].key)
                            transpose(ptw, Tile(wqt.ap[:, cs * 128:(cs + 1) * 128],
                                                wqt.key))
                            thh = Tile(rbh[0].ap[:, pp * 128:pp * 128 + 128],
                                       f"rbh0.{pp}")
                            tll = Tile(rbl[0].ap[:, pp * 128:pp * 128 + 128],
                                       f"rbl0.{pp}")
                            split(thh, tll, ptw)
                            rr0 = c0 + cs * 128
                            dma_raw("gpsimd", wh_[rr0:rr0 + 128, r * 128:(r + 1) * 128],
                                    thh.ap, reads=[thh.key], writes=["whbm"])
                            dma_raw("gpsimd", wl_[rr0:rr0 + 128, r * 128:(r + 1) * 128],
                                    tll.ap, reads=[tll.key], writes=["wlbm"])

            def ln_fm(load_chunk, use_ln, nchunks=KC):
                for k in range(nchunks):
                    xc = load_chunk(k)
                    split(rbh[0], rbl[0], xc)
                    sc_(lambda e, o=rot[1], i=xc: e.activation(
                            o.ap, i.ap, ACT.Square),
                        reads=[xc.key], writes=[rot[1].key])
                    split(rbh[1], rbl[1], rot[1])
                    for s in range(NSL):
                        st, sp = (k == 0), (k == nchunks - 1)
                        sl_ = slice(s * SL, (s + 1) * SL)
                        s1 = Tile(psbig.ap[:, sl_], f"psbig.{s}")
                        s2 = Tile(p512[s].ap, p512[s].key)
                        mm(s1, sONEb, Tile(rbh[0].ap[:, sl_], rbh[0].key), st, False)
                        mm(s1, sONEb, Tile(rbl[0].ap[:, sl_], rbl[0].key), False, sp)
                        mm(s2, sONEb, Tile(rbh[1].ap[:, sl_], rbh[1].key), st, False)
                        mm(s2, sONEb, Tile(rbl[1].ap[:, sl_], rbl[1].key), False, sp)
                a_t, b_t = rot[1], rot[2]
                for s in range(NSL):
                    sl_ = slice(s * SL, (s + 1) * SL)
                    s1 = Tile(psbig.ap[:, sl_], f"psbig.{s}")
                    s2 = Tile(p512[s].ap, p512[s].key)
                    mu = Tile(ab_a.ap, ab_a.key)
                    m2 = Tile(ab_b.ap, ab_b.key)
                    v(lambda e, o=mu, i=s1: e.tensor_scalar(
                          o.ap, i.ap, 1.0 / C, None, ALU.mult),
                      reads=[s1.key], writes=[mu.key])
                    v(lambda e, o=m2, i=s2: e.tensor_scalar(
                          o.ap, i.ap, 1.0 / C, None, ALU.mult),
                      reads=[s2.key], writes=[m2.key])
                    tv = Tile(rot[3].ap[:, sl_], rot[3].key)
                    v(lambda e, o=tv, a=mu: e.tensor_tensor(o.ap, a.ap, a.ap, ALU.mult),
                      reads=[mu.key], writes=[tv.key])
                    v(lambda e, o=tv, a=m2: e.tensor_tensor(o.ap, a.ap, o.ap, ALU.subtract),
                      reads=[m2.key, tv.key], writes=[tv.key])
                    av = Tile(a_t.ap[:, sl_], a_t.key)
                    sc_(lambda e, o=av, i=tv: e.activation(
                            o.ap, i.ap, ACT.Sqrt, bias=scal.ap[:, 14:15]),
                        reads=[tv.key, "sc14"], writes=[av.key])
                    v(lambda e, o=av: e.reciprocal(o.ap, o.ap),
                      reads=[av.key], writes=[av.key])
                    bv = Tile(b_t.ap[:, sl_], b_t.key)
                    v(lambda e, o=bv, a=mu, c=av: e.scalar_tensor_tensor(
                          o.ap, a.ap, -1.0, c.ap, ALU.mult, ALU.mult),
                      reads=[mu.key, av.key], writes=[bv.key])
                for k in range(nchunks):
                    xc = load_chunk(k)
                    lnc = rot[3]
                    v(lambda e, o=lnc, i=xc, a=a_t: e.tensor_tensor(
                          o.ap, i.ap, a.ap, ALU.mult),
                      reads=[xc.key, a_t.key], writes=[lnc.key])
                    v(lambda e, o=lnc, b=b_t: e.tensor_tensor(
                          o.ap, o.ap, b.ap, ALU.add),
                      reads=[lnc.key, b_t.key], writes=[lnc.key])
                    use_ln(k, lnc)

            bridge(['rot0.0', 'rot0.1', 'rot1.0', 'rot1.1', 'rot2.0', 'rot2.1', 'rot3.0', 'rot3.1', 'rbh0.0', 'rbh0.1', 'rbl0.0', 'rbl0.1', 'lmq.0', 'lmq.1', 'lmk.0', 'lmk.1', 'rint.0', 'rint.1'], ['rot0', 'rot1', 'rot2', 'rot3', 'rbh0', 'rbl0', 'lmq', 'lmk', 'rint'])

            for b in range(NB):
                def loadx0(k, b=b):
                    t = rot[0]
                    dma_raw("sync", t.ap, x0[b, k * 128:(k + 1) * 128, :],
                            reads=[], writes=[t.key])
                    return t

                def use_ln1(k, lnc):
                    h = Tile(pool[0].ap[:, k * 2048:(k + 1) * 2048], pool[0].key)
                    l_ = Tile(pool[1].ap[:, k * 2048:(k + 1) * 2048], pool[1].key)
                    split(h, l_, lnc)

                ln_fm(loadx0, use_ln1)

                for cc in range(24):
                    w1h, w1l = w1h2[cc % 2], w1l2[cc % 2]
                    for k in range(KC):
                        dma_raw("sync", w1h.ap[:, k * 128:(k + 1) * 128],
                                s_wq_h[k * 128:(k + 1) * 128, cc * 128:(cc + 1) * 128],
                                reads=["whbm"], writes=[w1h.key])
                        dma_raw("sync", w1l.ap[:, k * 128:(k + 1) * 128],
                                s_wq_l[k * 128:(k + 1) * 128, cc * 128:(cc + 1) * 128],
                                reads=["wlbm"], writes=[w1l.key])
                    scale = 0.125 if cc < 8 else 1.0
                    ev = rot[0] if cc % 2 == 0 else rot[2]
                    for s in range(NSL):
                        p = p512[s]
                        sl_ = slice(s * SL, (s + 1) * SL)
                        for k in range(KC):
                            lh = Tile(w1h.ap[:, k * 128:(k + 1) * 128], w1h.key)
                            ll = Tile(w1l.ap[:, k * 128:(k + 1) * 128], w1l.key)
                            rh = Tile(pool[0].ap[:, k * 2048 + s * SL:k * 2048 + (s + 1) * SL],
                                      pool[0].key)
                            rl = Tile(pool[1].ap[:, k * 2048 + s * SL:k * 2048 + (s + 1) * SL],
                                      pool[1].key)
                            mm3(p, lh, ll, rh, rl, k == 0, k == KC - 1)
                        dv = Tile(ev.ap[:, sl_], ev.key)
                        sc_(lambda e, o=dv, i=p, scv=scale: e.activation(
                                o.ap, i.ap, ACT.Copy, scale=scv),
                            reads=[p.key], writes=[ev.key])
                    dst_t = (s_q, s_k, s_vt)[cc // 8]
                    rr = (cc % 8) * 128
                    dma_raw("gpsimd", dst_t[b, rr:rr + 128, :], ev.ap,
                            reads=[ev.key], writes=["s_qkv"])

                t_q = pv(0, 0)
                t_k = pv(0, 2048)
                t_v = pv(0, 4096)
                t_vtok = pv(0, 6144)
                t_e2 = pv(1, 0)
                t_rd = pv(1, 2048)
                for j in range(8):
                    dma_raw("sync", t_q.ap, s_q[b, j * 128:(j + 1) * 128, :],
                            reads=["s_qkv"], writes=[pool[0].key])
                    dma_raw("sync", t_k.ap, s_k[b, j * 128:(j + 1) * 128, :],
                            reads=["s_qkv"], writes=[pool[0].key])
                    dma_raw("sync", t_v.ap, s_vt[b, j * 128:(j + 1) * 128, :],
                            reads=["s_qkv"], writes=[pool[0].key])
                    for s16 in range(16):
                        pt = Tile(p512[3].ap[:, 0:128], p512[3].key)
                        transpose(pt, Tile(t_v.ap[:, s16 * 128:(s16 + 1) * 128],
                                           pool[0].key))
                        vt_ = Tile(t_vtok.ap[:, s16 * 128:(s16 + 1) * 128], pool[0].key)
                        v(lambda e, o=vt_, i=pt: e.tensor_copy(o.ap, i.ap),
                          reads=[pt.key], writes=[pool[0].key])
                    v(lambda e, o=lmq, i=t_q: e.reduce_sum(
                          o.ap, i.ap.rearrange("p (g w) -> p g w", w=32), axis=AX),
                      reads=[pool[0].key], writes=[lmq.key])
                    v(lambda e, o=lmq: e.tensor_scalar(o.ap, o.ap, 1.0 / 32, None, ALU.mult),
                      reads=[lmq.key], writes=[lmq.key])
                    v(lambda e, o=lmk, i=t_k: e.reduce_sum(
                          o.ap, i.ap.rearrange("p (g w) -> p g w", w=32), axis=AX),
                      reads=[pool[0].key], writes=[lmk.key])
                    v(lambda e, o=lmk: e.tensor_scalar(o.ap, o.ap, 1.0 / 32, None, ALU.mult),
                      reads=[lmk.key], writes=[lmk.key])
                    for (bd, lm) in ((s128["blq"], lmq), (s128["blk"], lmk)):
                        v(lambda e, o=bd: e.memset(o.ap, 0.0), reads=[], writes=[bd.key])
                        v(lambda e, o=bd, i=lm: e.tensor_copy(
                              o.ap[0:64, 0:64], i.ap[0:64, :]),
                          reads=[lm.key], writes=[bd.key])
                        v(lambda e, o=bd, i=lm: e.tensor_copy(
                              o.ap[64:128, 64:128], i.ap[64:128, :]),
                          reads=[lm.key], writes=[bd.key])
                    pk2 = Tile(p512[0].ap[:, 0:128], p512[0].key)
                    mm(pk2, s128["blq"], s128["blk"], True, True)
                    kbd = s128["kbd"]
                    v(lambda e, o=kbd: e.memset(o.ap, 0.0), reads=[], writes=[kbd.key])
                    for blk0 in (0, 64):
                        v(lambda e, i=pk2, b0=blk0: e.reduce_max(
                              sm.ap[b0:b0 + 64, 12:13],
                              i.ap[b0:b0 + 64, b0:b0 + 64], axis=AX),
                          reads=[pk2.key], writes=[sm.key])
                        v(lambda e, b0=blk0: e.tensor_scalar(
                              sm.ap[b0:b0 + 64, 13:14], sm.ap[b0:b0 + 64, 12:13],
                              -1.0, None, ALU.mult),
                          reads=[sm.key], writes=[sm.key])
                        A("scalar",
                          lambda e, b0=blk0, i=pk2, o=kbd: e.activation(
                              o.ap[b0:b0 + 64, b0:b0 + 64],
                              i.ap[b0:b0 + 64, b0:b0 + 64],
                              ACT.Exp, bias=sm.ap[b0:b0 + 64, 13:14],
                              accum_out=sm.ap[b0:b0 + 64, 14:15]),
                          reads=[pk2.key, sm.key], writes=[kbd.key, sm.key])
                    v(lambda e: e.reciprocal(sm.ap[:, 15:16], sm.ap[:, 14:15]),
                      reads=[sm.key], writes=[sm.key])
                    v(lambda e, o=kbd: e.tensor_scalar(
                          o.ap, o.ap, sm.ap[:, 15:16], None, ALU.mult),
                      reads=[kbd.key, sm.key], writes=[kbd.key])
                    ktb = s128["ktb"]
                    ptt = Tile(p512[1].ap[:, 0:128], p512[1].key)
                    transpose(ptt, kbd)
                    v(lambda e, o=ktb, i=ptt: e.tensor_copy(o.ap, i.ap),
                      reads=[ptt.key], writes=[ktb.key])
                    v(lambda e, i=ktb: e.reduce_sum(sm.ap[:, 12:13], i.ap, axis=AX),
                      reads=[ktb.key], writes=[sm.key])
                    pt2 = Tile(p512[2].ap[0:1, 0:128], p512[2].key)
                    transpose(pt2, Tile(sm.ap[:, 12:13], sm.key))
                    v(lambda e, i=pt2: e.reduce_max(
                          svec.ap[0:1, 1:2], i.ap[0:1, 0:64], axis=AX),
                      reads=[pt2.key], writes=[svec.key])
                    v(lambda e, i=pt2: e.reduce_max(
                          svec.ap[0:1, 2:3], i.ap[0:1, 64:128], axis=AX),
                      reads=[pt2.key], writes=[svec.key])
                    dma_raw("gpsimd", s_sc[0:1, 1:3], svec.ap[0:1, 1:3],
                            reads=[svec.key], writes=["s_sc"])
                    dma_raw("gpsimd", scal.ap[0:64, 1:2],
                            s_sc[0:1, 1:2].partition_broadcast(64),
                            reads=["s_sc"], writes=["sc1"])
                    dma_raw("gpsimd", scal.ap[64:128, 1:2],
                            s_sc[0:1, 2:3].partition_broadcast(64),
                            reads=["s_sc"], writes=["sc1"])
                    v(lambda e: e.reciprocal(scal.ap[:, 1:2], scal.ap[:, 1:2]),
                      reads=["sc1"], writes=["sc1"])
                    vv, vt = s128["vv"], s128["vt"]
                    v(lambda e, o=vv, i=ktb: e.tensor_scalar(
                          o.ap, i.ap, scal.ap[:, 1:2], None, ALU.mult),
                      reads=[ktb.key, "sc1"], writes=[vv.key])
                    ptt2 = Tile(p512[0].ap[:, 0:128], p512[0].key)
                    transpose(ptt2, vv)
                    v(lambda e, o=vt, i=ptt2: e.tensor_copy(o.ap, i.ap),
                      reads=[ptt2.key], writes=[vt.key])
                    kv, kvt, ta = s128["kv"], s128["kvt"], s128["ta"]

                    def inv_iter():
                        pI = Tile(p512[1].ap[:, 0:128], p512[1].key)
                        mm(pI, ktb, vv, True, True)
                        v(lambda e, o=kv, i=pI: e.tensor_copy(o.ap, i.ap),
                          reads=[pI.key], writes=[kv.key])
                        pT = Tile(p512[2].ap[:, 0:128], p512[2].key)
                        transpose(pT, kv)
                        v(lambda e, o=kvt, i=pT: e.tensor_copy(o.ap, i.ap),
                          reads=[pT.key], writes=[kvt.key])
                        v(lambda e, o=ta, i=kv: e.scalar_tensor_tensor(
                              o.ap, i.ap, -1.0, s128["sD7"].ap, ALU.mult, ALU.add),
                          reads=[kv.key, "t128_sD7"], writes=[ta.key])
                        pI2 = Tile(p512[1].ap[:, 0:128], p512[1].key)
                        mm(pI2, kvt, ta, True, True)
                        v(lambda e, o=ta, i=pI2: e.scalar_tensor_tensor(
                              o.ap, i.ap, -1.0, s128["sD15"].ap, ALU.mult, ALU.add),
                          reads=[pI2.key, "t128_sD15"], writes=[ta.key])
                        pI3 = Tile(p512[1].ap[:, 0:128], p512[1].key)
                        mm(pI3, kvt, ta, True, True)
                        v(lambda e, o=ta, i=pI3: e.scalar_tensor_tensor(
                              o.ap, i.ap, -0.25, s128["sD325"].ap, ALU.mult, ALU.add),
                          reads=[pI3.key, "t128_sD325"], writes=[ta.key])
                        pI4 = Tile(p512[1].ap[:, 0:128], p512[1].key)
                        mm(pI4, vt, ta, True, True)
                        v(lambda e, o=vv, i=pI4: e.tensor_copy(o.ap, i.ap),
                          reads=[pI4.key], writes=[vv.key])
                        pT2 = Tile(p512[2].ap[:, 0:128], p512[2].key)
                        transpose(pT2, vv)
                        v(lambda e, o=vt, i=pT2: e.tensor_copy(o.ap, i.ap),
                          reads=[pT2.key], writes=[vt.key])
                    if b == 0 and j == 0:
                        tap("kbd", kbd)
                        tap("ktb", ktb)
                        tap("v6", vv)
                        tap("v6t", vt)

                    for s in range(NSL):
                        s1 = Tile(psbig.ap[:, s * SL:(s + 1) * SL], f"psbig.{s}")
                        mm(s1, s128["blk"], Tile(t_q.ap[:, s * SL:(s + 1) * SL],
                                                 pool[0].key), True, True)
                    mx = Tile(sm.ap[:, 0:1], sm.key)
                    for s in range(NSL):
                        s1 = Tile(psbig.ap[:, s * SL:(s + 1) * SL], f"psbig.{s}")
                        if s == 0:
                            v(lambda e, o=mx, i=s1: e.reduce_max(o.ap, i.ap, axis=AX),
                              reads=[s1.key], writes=[sm.key])
                        else:
                            v(lambda e, i=s1: e.reduce_max(sm.ap[:, 1:2], i.ap, axis=AX),
                              reads=[s1.key], writes=[sm.key])
                            v(lambda e: e.tensor_tensor(
                                  sm.ap[:, 0:1], sm.ap[:, 0:1], sm.ap[:, 1:2], ALU.max),
                              reads=[sm.key], writes=[sm.key])
                    inv_iter()
                    pmax_scalar(mx, 0, 0)
                    v(lambda e: e.tensor_scalar(scal.ap[:, 0:1], scal.ap[:, 0:1],
                                                -1.0, None, ALU.mult),
                      reads=["sc0"], writes=["sc0"])
                    for s in range(NSL):
                        s1 = Tile(psbig.ap[:, s * SL:(s + 1) * SL], f"psbig.{s}")
                        e2s = Tile(t_e2.ap[:, s * SL:(s + 1) * SL], pool[1].key)
                        sc_(lambda e, o=e2s, i=s1: e.activation(
                                o.ap, i.ap, ACT.Exp, bias=scal.ap[:, 0:1]),
                            reads=[s1.key, "sc0"], writes=[pool[1].key])
                    inv_iter()
                    for s in range(NSL):
                        p = p512[0]
                        e2s = Tile(t_e2.ap[:, s * SL:(s + 1) * SL], pool[1].key)
                        mm(p, s128["sBON"], e2s, True, True)
                        rds = Tile(t_rd.ap[:, s * SL:(s + 1) * SL], pool[1].key)
                        v(lambda e, o=rds, i=p: e.reciprocal(o.ap, i.ap),
                          reads=[p.key], writes=[pool[1].key])
                    inv_iter()
                    for s in range(NSL):
                        p = p512[1]
                        mm(p, s128["blq"], Tile(t_k.ap[:, s * SL:(s + 1) * SL],
                                                pool[0].key), True, True)
                        k3s = Tile(t_v.ap[:, s * SL:(s + 1) * SL], pool[0].key)
                        v(lambda e, o=k3s, i=p: e.tensor_copy(o.ap, i.ap),
                          reads=[p.key], writes=[pool[0].key])
                        v(lambda e, i=p, s=s: e.reduce_max(
                              sm.ap[:, 2 + s:3 + s], i.ap, axis=AX),
                          reads=[p.key], writes=[sm.key])
                    for s in range(1, NSL):
                        v(lambda e, s=s: e.tensor_tensor(
                              sm.ap[:, 2:3], sm.ap[:, 2:3],
                              sm.ap[:, 2 + s:3 + s], ALU.max),
                          reads=[sm.key], writes=[sm.key])
                    v(lambda e: e.tensor_scalar(sm.ap[:, 6:7], sm.ap[:, 2:3],
                                                -1.0, None, ALU.mult),
                      reads=[sm.key], writes=[sm.key])
                    for s in range(NSL):
                        k3s = Tile(t_v.ap[:, s * SL:(s + 1) * SL], pool[0].key)
                        A("scalar",
                          lambda e, o=k3s, s=s: e.activation(
                              o.ap, o.ap, ACT.Exp, bias=sm.ap[:, 6:7],
                              accum_out=sm.ap[:, 8 + s:9 + s]),
                          reads=[pool[0].key, sm.key], writes=[pool[0].key, sm.key])
                    for s in range(1, NSL):
                        v(lambda e, s=s: e.tensor_tensor(
                              sm.ap[:, 8:9], sm.ap[:, 8:9],
                              sm.ap[:, 8 + s:9 + s], ALU.add),
                          reads=[sm.key], writes=[sm.key])
                    v(lambda e: e.reciprocal(sm.ap[:, 7:8], sm.ap[:, 8:9]),
                      reads=[sm.key], writes=[sm.key])
                    v(lambda e, o=t_v: e.tensor_scalar(
                          o.ap, o.ap, sm.ap[:, 7:8], None, ALU.mult),
                      reads=[pool[0].key, sm.key], writes=[pool[0].key])
                    inv_iter()
                    inv_iter()
                    inv_iter()
                    for s16 in range(16):
                        pt = Tile(p512[2].ap[:, 0:128], p512[2].key)
                        transpose(pt, Tile(t_v.ap[:, s16 * 128:(s16 + 1) * 128],
                                           pool[0].key))
                        tr = s128["tr1"]
                        v(lambda e, o=tr, i=pt: e.tensor_copy(o.ap, i.ap),
                          reads=[pt.key], writes=[tr.key])
                        pk = Tile(p512[3].ap[:, 0:128], p512[3].key)
                        mm(pk, tr, Tile(t_vtok.ap[:, s16 * 128:(s16 + 1) * 128],
                                        pool[0].key), s16 == 0, s16 == 15)
                    kv3 = s128["k3v"]
                    v(lambda e, o=kv3: e.memset(o.ap, 0.0), reads=[], writes=[kv3.key])
                    pk = Tile(p512[3].ap[:, 0:128], p512[3].key)
                    v(lambda e, o=kv3, i=pk: e.tensor_copy(
                          o.ap[0:64, 0:64], i.ap[0:64, 0:64]),
                      reads=[pk.key], writes=[kv3.key])
                    v(lambda e, o=kv3, i=pk: e.tensor_copy(
                          o.ap[64:128, 64:128], i.ap[64:128, 64:128]),
                      reads=[pk.key], writes=[kv3.key])
                    if b == 0 and j == 0:
                        tap("e2", Tile(t_e2.ap, pool[1].key))
                        tap("rd", Tile(t_rd.ap, pool[1].key))
                        tap("k3n", Tile(t_v.ap, pool[0].key))
                        tap("kv3", kv3)
                        tap("blq", s128["blq"])
                        tap("blk", s128["blk"])
                    pM = Tile(p512[1].ap[:, 0:128], p512[1].key)
                    mm(pM, vt, kv3, True, True)
                    m2t = s128["m2"]
                    v(lambda e, o=m2t, i=pM: e.tensor_copy(o.ap, i.ap),
                      reads=[pM.key], writes=[m2t.key])
                    for s in range(NSL):
                        s1 = Tile(psbig.ap[:, s * SL:(s + 1) * SL], f"psbig.{s}")
                        mm(s1, m2t, Tile(t_e2.ap[:, s * SL:(s + 1) * SL],
                                         pool[1].key), True, True)
                        tmp = Tile(rot[0].ap[:, 0:SL], rot[0].key)
                        v(lambda e, o=tmp, i=s1,
                              r=Tile(t_rd.ap[:, s * SL:(s + 1) * SL], pool[1].key):
                              e.tensor_tensor(o.ap, i.ap, r.ap, ALU.mult),
                          reads=[s1.key, pool[1].key], writes=[rot[0].key])
                        if b == 0 and j == 0 and s == 0:
                            tap("m2", m2t)
                            tap("svt0", tmp)
                        off = j * 2048 + s * SL
                        hh = Tile(pool[2].ap[:, off:off + SL], pool[2].key)
                        ll2 = Tile(pool[3].ap[:, off:off + SL], pool[3].key)
                        split(hh, ll2, tmp)

                for o in range(8):
                    w1h, w1l = w1h2[o % 2], w1l2[o % 2]
                    for k in range(KC):
                        dma_raw("sync", w1h.ap[:, k * 128:(k + 1) * 128],
                                s_wp_h[k * 128:(k + 1) * 128, o * 128:(o + 1) * 128],
                                reads=["whbm"], writes=[w1h.key])
                        dma_raw("sync", w1l.ap[:, k * 128:(k + 1) * 128],
                                s_wp_l[k * 128:(k + 1) * 128, o * 128:(o + 1) * 128],
                                reads=["wlbm"], writes=[w1l.key])
                    for s in range(NSL):
                        p = p512[s]
                        for k in range(KC):
                            lh = Tile(w1h.ap[:, k * 128:(k + 1) * 128], w1h.key)
                            ll = Tile(w1l.ap[:, k * 128:(k + 1) * 128], w1l.key)
                            off = k * 2048 + s * SL
                            rh = Tile(pool[2].ap[:, off:off + SL], pool[2].key)
                            rl = Tile(pool[3].ap[:, off:off + SL], pool[3].key)
                            mm3(p, lh, ll, rh, rl, k == 0, k == KC - 1)
                        pf = poolf[o // 4]
                        off2 = (o % 4) * 2048 + s * SL
                        dst = Tile(pf.ap[:, off2:off2 + SL], pf.key)
                        sc_(lambda e, o2=dst, i=p: e.activation(o2.ap, i.ap, ACT.Copy),
                            reads=[p.key], writes=[pf.key])

                if b == 0:
                    tap("proj0", Tile(poolf[0].ap[:, 0:2048], pool[0].key))

                def loadproj(k):
                    pf = poolf[k // 4]
                    off = (k % 4) * 2048
                    return Tile(pf.ap[:, off:off + 2048], pf.key)

                def use_ln2(k, lnc, b=b):
                    xr = rot[0]
                    dma_raw("sync", xr.ap, x0[b, k * 128:(k + 1) * 128, :],
                            reads=[], writes=[xr.key])
                    v(lambda e, o=lnc, i=xr: e.tensor_tensor(o.ap, o.ap, i.ap, ALU.add),
                      reads=[lnc.key, xr.key], writes=[lnc.key])
                    dma_raw("gpsimd", s_x1[b, k * 128:(k + 1) * 128, :], lnc.ap,
                            reads=[lnc.key], writes=["s_x1"])

                ln_fm(loadproj, use_ln2)

            v(lambda e: e.memset(scal.ap[:, 13:14], 0.0), reads=[], writes=["sc13"])
            for b in range(NB):
                def loadx1(k, b=b):
                    t = rot[0]
                    dma_raw("sync", t.ap, s_x1[b, k * 128:(k + 1) * 128, :],
                            reads=["s_x1"], writes=[t.key])
                    return t

                def use_ln3(k, lnc, b=b):
                    dma_raw("gpsimd", s_ln3[b, k * 128:(k + 1) * 128, :], lnc.ap,
                            reads=[lnc.key], writes=["s_ln3"])
                    v(lambda e, i=lnc: e.reduce_max(sm.ap[:, 0:1], i.ap, axis=AX,
                                                    apply_absolute_value=True),
                      reads=[lnc.key], writes=[sm.key])
                    v(lambda e: e.tensor_tensor(scal.ap[:, 13:14], scal.ap[:, 13:14],
                                                sm.ap[:, 0:1], ALU.max),
                      reads=[sm.key, "sc13"], writes=["sc13"])

                ln_fm(loadx1, use_ln3)

            pmax_scalar(col(13), 3, 12)
            allreduce_max(0, 12, 12)
            v(lambda e: e.tensor_scalar(scal.ap[:, 2:3], scal.ap[:, 12:13],
                                        1.0 / QMAX, 1e-12, ALU.mult, ALU.max),
              reads=["sc2", "sc12"], writes=["sc2", "sc12"])
            v(lambda e: e.reciprocal(scal.ap[:, 3:4], scal.ap[:, 2:3]),
              reads=["sc2", "sc3"], writes=["sc2", "sc3"])

            for b in range(NB):
                for k in range(KC):
                    t = rot[0]
                    dma_raw("sync", t.ap, s_ln3[b, k * 128:(k + 1) * 128, :],
                            reads=["s_ln3"], writes=[t.key])
                    z = rot[1]
                    sc_(lambda e, o=z, i=t: e.activation(o.ap, i.ap, ACT.Copy,
                                                         scale=scal.ap[:, 3:4]),
                        reads=[t.key, "sc3"], writes=[z.key])
                    R = rot[2]
                    round_ip(R, z, rot[3], z)
                    split(rbh[0], rbl[0], R)
                    c0 = b * NT
                    dma_raw("gpsimd", s_Rh[k * 128:(k + 1) * 128, c0:c0 + NT],
                            rbh[0].ap, reads=[rbh[0].key], writes=["s_Rh"])
                    dma_raw("gpsimd", s_Rl[k * 128:(k + 1) * 128, c0:c0 + NT],
                            rbl[0].ap, reads=[rbl[0].key], writes=["s_Rl"])

            def b2_part(wh_hbm, wl_hbm, dst_hbm, dst_key, extra_work=None):
                v(lambda e: e.memset(scal.ap[:, 13:14], 0.0),
                  reads=[], writes=["sc13"])
                for blk in range(2):
                    t0 = blk * 4096
                    for k in range(KC):
                        ph = pool[0] if k < 4 else pool[1]
                        pl = pool[2] if k < 4 else pool[3]
                        off = (k % 4) * 4096
                        dma_raw("sync", ph.ap[:, off:off + 4096],
                                s_Rh[k * 128:(k + 1) * 128, t0:t0 + 4096],
                                reads=["s_Rh"], writes=[ph.key])
                        dma_raw("sync", pl.ap[:, off:off + 4096],
                                s_Rl[k * 128:(k + 1) * 128, t0:t0 + 4096],
                                reads=["s_Rl"], writes=[pl.key])
                    for h in range(HC):
                        if extra_work is not None:
                            for _ in range(4):
                                extra_work()
                        w1h, w1l = w1h2[h % 2], w1l2[h % 2]
                        for k in range(KC):
                            dma_raw("sync", w1h.ap[:, k * 128:(k + 1) * 128],
                                    wh_hbm[k * 128:(k + 1) * 128, h * 128:(h + 1) * 128],
                                    reads=["whbm"], writes=[w1h.key])
                            dma_raw("sync", w1l.ap[:, k * 128:(k + 1) * 128],
                                    wl_hbm[k * 128:(k + 1) * 128, h * 128:(h + 1) * 128],
                                    reads=["wlbm"], writes=[w1l.key])
                        for s in range(8):
                            p = p512[s % 4]
                            for k in range(KC):
                                ph = pool[0] if k < 4 else pool[1]
                                pl = pool[2] if k < 4 else pool[3]
                                off = (k % 4) * 4096 + s * SL
                                lh = Tile(w1h.ap[:, k * 128:(k + 1) * 128], w1h.key)
                                ll = Tile(w1l.ap[:, k * 128:(k + 1) * 128], w1l.key)
                                rh = Tile(ph.ap[:, off:off + SL], ph.key)
                                rl = Tile(pl.ap[:, off:off + SL], pl.key)
                                mm3(p, lh, ll, rh, rl, k == 0, k == KC - 1)
                            evt = (ab_a, ab_b)[s % 2]
                            gt = Tile(evt.ap, evt.key)
                            sc_(lambda e, o=gt, i=p: e.activation(
                                    o.ap, i.ap, ACT.Copy, scale=scal.ap[:, 2:3]),
                                reads=[p.key, "sc2"], writes=[evt.key])
                            v(lambda e, i=gt: e.reduce_max(
                                  sm.ap[:, 1:2], i.ap, axis=AX,
                                  apply_absolute_value=True),
                              reads=[evt.key], writes=["b2sm"])
                            v(lambda e: e.tensor_tensor(
                                  scal.ap[:, 13:14], scal.ap[:, 13:14],
                                  sm.ap[:, 1:2], ALU.max),
                              reads=["b2sm", "sc13"], writes=["sc13"])
                            dma_raw("gpsimd",
                                    dst_hbm[h * 128:(h + 1) * 128,
                                            t0 + s * SL:t0 + (s + 1) * SL],
                                    gt.ap, reads=[evt.key], writes=[dst_key])

            b2_part(s_wg_h, s_wg_l, s_gate, "s_gate")
            pmax_scalar(col(13), 4, 12)
            allreduce_max(1, 12, 12)
            v(lambda e: e.tensor_scalar(scal.ap[:, 4:5], scal.ap[:, 12:13],
                                        1.0 / QMAX, 1e-12, ALU.mult, ALU.max),
              reads=["sc4", "sc12"], writes=["sc4", "sc12"])
            v(lambda e: e.reciprocal(scal.ap[:, 5:6], scal.ap[:, 4:5]),
              reads=["sc4", "sc5"], writes=["sc4", "sc5"])

            v(lambda e: e.memset(scal.ap[:, 15:16], 0.0), reads=[], writes=["sc15"])
            ntile = HC * NB

            rbh0f = Tile(rbh[0].ap.bitcast(F32), "x")
            rbl0f = Tile(rbl[0].ap.bitcast(F32), "x")
            rbh1f = Tile(rbh[1].ap.bitcast(F32), "x")
            rbl1f = Tile(rbl[1].ap.bitcast(F32), "x")

            def b3_tile(ti):
                hb, half = divmod(ti, 2)
                h, b = divmod(hb, NB)
                r0 = h * 128
                c0 = b * NT + half * 1024
                W = 1024
                pp = ti % 2
                if pp == 0:
                    t = Tile(rot[1].ap[:, 0:W], "b3a_t")
                    z = Tile(rot[1].ap[:, W:2 * W], "b3a_z")
                    Rg = Tile(rot[2].ap[:, 0:W], "b3a_rg")
                    rf = Tile(rot[2].ap[:, W:2 * W], "b3a_rf")
                    Rs = Tile(rot[3].ap[:, 0:W], "b3a_rs")
                    prod = Tile(rot[3].ap[:, W:2 * W], "b3a_pr")
                else:
                    t = Tile(rbh0f.ap[:, 0:W], "b3b_t")
                    z = Tile(rbl0f.ap[:, 0:W], "b3b_z")
                    Rg = Tile(rbh1f.ap[:, 0:W], "b3b_rg")
                    rf = Tile(rbl1f.ap[:, 0:W], "b3b_rf")
                    Rs = Tile(rot[0].ap[:, 0:W], "b3b_rs")
                    prod = Tile(rot[0].ap[:, W:2 * W], "b3b_pr")
                smx = f"b3sm{pp}"
                dma_raw("sync", t.ap, s_gate[r0:r0 + 128, c0:c0 + W],
                        reads=["s_gate"], writes=[t.key])
                sc_(lambda e, o=z, i=t: e.activation(o.ap, i.ap, ACT.Copy,
                                                     scale=scal.ap[:, 5:6]),
                    reads=[t.key, "sc5"], writes=[z.key])
                round_ip2(Rg, z, rf, z, pp)
                sgm = t
                sc_(lambda e, o=sgm, i=Rg: e.activation(o.ap, i.ap, ACT.Sigmoid,
                                                        scale=scal.ap[:, 4:5]),
                    reads=[Rg.key, "sc4"], writes=[sgm.key])
                z2 = z
                sc_(lambda e, o=z2, i=sgm: e.activation(o.ap, i.ap, ACT.Copy,
                                                        scale=65536.0),
                    reads=[sgm.key], writes=[z2.key])
                round_ip2(Rs, z2, rf, z2, pp)
                v(lambda e, o=Rs: e.tensor_scalar(o.ap, o.ap, 65535.0, None, ALU.min),
                  reads=[Rs.key], writes=[Rs.key])
                v(lambda e, o=prod, a=Rg, c=Rs: e.tensor_tensor(
                      o.ap, a.ap, c.ap, ALU.mult),
                  reads=[Rg.key, Rs.key], writes=[prod.key])
                v(lambda e, i=prod, sx=4 + pp: e.reduce_max(
                      sm.ap[:, sx:sx + 1], i.ap, axis=AX,
                      apply_absolute_value=True),
                  reads=[prod.key], writes=[smx])
                v(lambda e, sx=4 + pp: e.tensor_tensor(
                      scal.ap[:, 15:16], scal.ap[:, 15:16],
                      sm.ap[:, sx:sx + 1], ALU.max),
                  reads=[smx, "sc15"], writes=["sc15"])
                dma_raw("gpsimd", s_g2r[r0:r0 + 128, c0:c0 + W], prod.ap,
                        reads=[prod.key], writes=["s_g2r"])

            bridge(['rot0', 'rot1', 'rot2', 'rot3', 'rbh0', 'rbl0', 'rbh1', 'rbl1', 'rint', 'sm'], ['b3a_t', 'b3a_z', 'b3a_rg', 'b3a_rf', 'b3a_rs', 'b3a_pr', 'b3b_t', 'b3b_z', 'b3b_rg', 'b3b_rf', 'b3b_rs', 'b3b_pr', 'rint.0', 'rint.1', 'b3sm0', 'b3sm1', 'b2sm'])
            b3_iter = iter(range(2 * ntile))

            def b3_work():
                ti = next(b3_iter, None)
                if ti is not None:
                    b3_tile(ti)

            b2_part(s_wu_h, s_wu_l, s_up, "s_up", extra_work=b3_work)
            for _ in range(2 * ntile):
                b3_work()
            bridge(['b3a_t', 'b3a_z', 'b3a_rg', 'b3a_rf', 'b3a_rs', 'b3a_pr', 'b3b_t', 'b3b_z', 'b3b_rg', 'b3b_rf', 'b3b_rs', 'b3b_pr', 'rint.0', 'rint.1', 'b3sm0', 'b3sm1', 'b2sm'], ['rot0', 'rot1', 'rot2', 'rot3', 'rbh0', 'rbl0', 'rbh1', 'rbl1', 'rint', 'sm'])
            pmax_scalar(col(13), 5, 12)
            allreduce_max(2, 12, 12)
            v(lambda e: e.tensor_scalar(scal.ap[:, 6:7], scal.ap[:, 12:13],
                                        1.0 / QMAX, 1e-12, ALU.mult, ALU.max),
              reads=["sc6", "sc12"], writes=["sc6", "sc12"])
            v(lambda e: e.reciprocal(scal.ap[:, 7:8], scal.ap[:, 6:7]),
              reads=["sc6", "sc7"], writes=["sc6", "sc7"])

            v(lambda e: e.tensor_scalar(scal.ap[:, 13:14], scal.ap[:, 15:16],
                                        scal.ap[:, 4:5], 1.0 / 65536.0,
                                        ALU.mult, ALU.mult),
              reads=["sc4", "sc13", "sc15"], writes=["sc4", "sc13", "sc15"])
            pmax_scalar(col(13), 6, 12)
            allreduce_max(3, 12, 12)
            v(lambda e: e.tensor_scalar(scal.ap[:, 9:10], scal.ap[:, 12:13],
                                        1.0 / QMAX, 1e-12, ALU.mult, ALU.max),
              reads=["sc9", "sc12"], writes=["sc9", "sc12"])
            v(lambda e: e.reciprocal(scal.ap[:, 12:13], scal.ap[:, 9:10]),
              reads=["sc9", "sc12"], writes=["sc9", "sc12"])
            v(lambda e: e.tensor_scalar(scal.ap[:, 8:9], scal.ap[:, 4:5],
                                        1.0 / 65536.0, None, ALU.mult),
              reads=["sc4", "sc8"], writes=["sc4", "sc8"])
            v(lambda e: e.tensor_tensor(scal.ap[:, 8:9], scal.ap[:, 8:9],
                                        scal.ap[:, 12:13], ALU.mult),
              reads=["sc8", "sc12"], writes=["sc8", "sc12"])

            v(lambda e: e.memset(scal.ap[:, 13:14], 0.0), reads=[], writes=["sc13"])
            for ti in range(ntile):
                h, b = divmod(ti, NB)
                r0, c0 = h * 128, b * NT
                tg = rot[0]
                dma_raw("sync", tg.ap, s_g2r[r0:r0 + 128, c0:c0 + NT],
                        reads=["s_g2r"], writes=[tg.key])
                z = rot[1]
                sc_(lambda e, o=z, i=tg: e.activation(o.ap, i.ap, ACT.Copy,
                                                      scale=scal.ap[:, 8:9]),
                    reads=[tg.key, "sc8"], writes=[z.key])
                Rg2 = rot[2]
                round_ip(Rg2, z, rot[3], z)
                tu = rot[0]
                dma_raw("sync", tu.ap, s_up[r0:r0 + 128, c0:c0 + NT],
                        reads=["s_up"], writes=[tu.key])
                z2 = rot[1]
                sc_(lambda e, o=z2, i=tu: e.activation(o.ap, i.ap, ACT.Copy,
                                                       scale=scal.ap[:, 7:8]),
                    reads=[tu.key, "sc7"], writes=[z2.key])
                Ru = rot[3]
                round_ip(Ru, z2, rot[0], z2)
                O_ = rot[1]
                v(lambda e, o=O_, a=Rg2, c=Ru: e.tensor_tensor(
                      o.ap, a.ap, c.ap, ALU.mult),
                  reads=[Rg2.key, Ru.key], writes=[O_.key])
                v(lambda e, i=O_: e.reduce_max(sm.ap[:, 1:2], i.ap, axis=AX,
                                               apply_absolute_value=True),
                  reads=[O_.key], writes=[sm.key])
                v(lambda e: e.tensor_tensor(scal.ap[:, 13:14], scal.ap[:, 13:14],
                                            sm.ap[:, 1:2], ALU.max),
                  reads=[sm.key, "sc13"], writes=["sc13"])
                dma_raw("gpsimd", s_or[r0:r0 + 128, c0:c0 + NT], O_.ap,
                        reads=[O_.key], writes=["s_or"])
            v(lambda e: e.tensor_scalar(scal.ap[:, 13:14], scal.ap[:, 13:14],
                                        scal.ap[:, 9:10], None, ALU.mult),
              reads=["sc9", "sc13"], writes=["sc9", "sc13"])
            v(lambda e: e.tensor_tensor(scal.ap[:, 13:14], scal.ap[:, 13:14],
                                        scal.ap[:, 6:7], ALU.mult),
              reads=["sc6", "sc13"], writes=["sc6", "sc13"])
            pmax_scalar(col(13), 7, 12)
            allreduce_max(4, 12, 12)
            v(lambda e: e.tensor_scalar(scal.ap[:, 11:12], scal.ap[:, 12:13],
                                        1.0 / QMAX, 1e-12, ALU.mult, ALU.max),
              reads=["sc11", "sc12"], writes=["sc11", "sc12"])
            v(lambda e: e.reciprocal(scal.ap[:, 12:13], scal.ap[:, 11:12]),
              reads=["sc11", "sc12"], writes=["sc11", "sc12"])
            v(lambda e: e.tensor_tensor(scal.ap[:, 10:11], scal.ap[:, 9:10],
                                        scal.ap[:, 6:7], ALU.mult),
              reads=["sc6", "sc9", "sc10"], writes=["sc6", "sc9", "sc10"])
            v(lambda e: e.tensor_tensor(scal.ap[:, 10:11], scal.ap[:, 10:11],
                                        scal.ap[:, 12:13], ALU.mult),
              reads=["sc10", "sc12"], writes=["sc10", "sc12"])

            for h in range(HC):
                ph = pool[0] if h < 16 else pool[1]
                pl = pool[2] if h < 16 else pool[3]
                off = (h % 16) * 1024
                dma_raw("sync", ph.ap[:, off:off + 1024],
                        s_wd_h[h * 128:(h + 1) * 128, :],
                        reads=["whbm"], writes=[ph.key, "wdres"])
                dma_raw("sync", pl.ap[:, off:off + 1024],
                        s_wd_l[h * 128:(h + 1) * 128, :],
                        reads=["wlbm"], writes=[pl.key, "wdres"])

            BSL = 256

            def b5pre_tile(h, b):
                r0, c0 = h * 128, b * NT
                t = rot[0]
                dma_raw("sync", t.ap, s_or[r0:r0 + 128, c0:c0 + NT],
                        reads=["s_or"], writes=[t.key])
                z = rot[1]
                sc_(lambda e, o=z, i=t: e.activation(o.ap, i.ap, ACT.Copy,
                                                     scale=scal.ap[:, 10:11]),
                    reads=[t.key, "sc10"], writes=[z.key])
                Ro = rot[2]
                round_ip(Ro, z, rot[3], z)
                split(rbh[0], rbl[0], Ro)
                dma_raw("gpsimd", s_roh[r0:r0 + 128, c0:c0 + NT], rbh[0].ap,
                        reads=[rbh[0].key], writes=["s_roh"])
                dma_raw("gpsimd", s_rol[r0:r0 + 128, c0:c0 + NT], rbl[0].ap,
                        reads=[rbl[0].key], writes=["s_rol"])

            def b5_slice(b, si):
                c0 = b * NT + si * BSL

                def dn_t(o):
                    pf = poolf[1] if o < 4 else poolf[3]
                    off = 7168 + (o % 4) * BSL
                    return Tile(pf.ap[:, off:off + BSL], f"dnv{o}")

                def pso(o):
                    if o < 4:
                        return Tile(psbig.ap[:, o * 512:o * 512 + BSL], f"psbig.{o}")
                    return Tile(p512[o - 4].ap[:, 0:BSL], p512[o - 4].key)

                for h in range(HC):
                    rh = robh[h % 2]
                    rl = robl[h % 2]
                    dma_raw("sync", rh.ap,
                            s_roh[h * 128:(h + 1) * 128, c0:c0 + BSL],
                            reads=["s_roh"], writes=[rh.key])
                    dma_raw("sync", rl.ap,
                            s_rol[h * 128:(h + 1) * 128, c0:c0 + BSL],
                            reads=["s_rol"], writes=[rl.key])
                    ph = pool[0] if h < 16 else pool[1]
                    pl = pool[2] if h < 16 else pool[3]
                    for o in range(8):
                        woff = (h % 16) * 1024 + o * 128
                        lh = Tile(ph.ap[:, woff:woff + 128], "wdres")
                        ll = Tile(pl.ap[:, woff:woff + 128], "wdres")
                        mm3(pso(o), lh, ll, rh, rl, h == 0, h == HC - 1)
                for o in range(8):
                    dnt = dn_t(o)
                    p = pso(o)
                    sc_(lambda e, o2=dnt, i=p: e.activation(
                            o2.ap, i.ap, ACT.Copy, scale=scal.ap[:, 11:12]),
                        reads=[p.key, "sc11"], writes=[dnt.key])
                s1 = Tile(psbig.ap[:, 0:BSL], "psbig.0")
                s2 = Tile(p512[0].ap[:, 0:BSL], p512[0].key)
                for o in range(8):
                    dnt = dn_t(o)
                    hh = Tile(rbh[1].ap[:, 0:BSL], rbh[1].key)
                    ll_ = Tile(rbl[1].ap[:, 0:BSL], rbl[1].key)
                    split(hh, ll_, dnt)
                    sqt = Tile(ab_b.ap[:, 256:512], ab_b.key)
                    sc_(lambda e, o2=sqt, i=dnt: e.activation(
                            o2.ap, i.ap, ACT.Square),
                        reads=[dnt.key], writes=[ab_b.key])
                    hh2 = Tile(rbh[1].ap[:, 256:512], rbh[1].key)
                    ll2 = Tile(rbl[1].ap[:, 256:512], rbl[1].key)
                    split(hh2, ll2, sqt)
                    mm(s1, sONEb, hh, o == 0, False)
                    mm(s1, sONEb, ll_, False, o == 7)
                    mm(s2, sONEb, hh2, o == 0, False)
                    mm(s2, sONEb, ll2, False, o == 7)
                mu = Tile(ab_a.ap[:, 0:BSL], ab_a.key)
                m2_ = Tile(psbig.ap[:, 1024:1280], "psbig.2")
                tv = Tile(ab_b.ap[:, 256:512], ab_b.key)
                av = Tile(psbig.ap[:, 1792:2048], "psbig.3")
                bv = Tile(psbig.ap[:, 1536:1792], "psbig.3")
                v(lambda e, o=mu, i=s1: e.tensor_scalar(
                      o.ap, i.ap, 1.0 / C, None, ALU.mult),
                  reads=["psbig.0"], writes=[ab_a.key])
                v(lambda e, o=m2_, i=s2: e.tensor_scalar(
                      o.ap, i.ap, 1.0 / C, None, ALU.mult),
                  reads=[p512[0].key], writes=["psbig.2"])
                v(lambda e, o=tv, a=mu: e.tensor_tensor(o.ap, a.ap, a.ap, ALU.mult),
                  reads=[ab_a.key], writes=[ab_b.key])
                v(lambda e, o=tv, a=m2_: e.tensor_tensor(o.ap, a.ap, o.ap, ALU.subtract),
                  reads=["psbig.2", ab_b.key], writes=[ab_b.key])
                sc_(lambda e, o=av, i=tv: e.activation(
                        o.ap, i.ap, ACT.Sqrt, bias=scal.ap[:, 14:15]),
                    reads=[ab_b.key, "sc14"], writes=["psbig.3"])
                v(lambda e, o=av: e.reciprocal(o.ap, o.ap),
                  reads=["psbig.3"], writes=["psbig.3"])
                v(lambda e, o=bv, a=mu, c=av: e.scalar_tensor_tensor(
                      o.ap, a.ap, -1.0, c.ap, ALU.mult, ALU.mult),
                  reads=[ab_a.key, "psbig.3"], writes=["psbig.3"])
                for o in range(8):
                    dnt = dn_t(o)
                    yv = Tile(ab_a.ap[:, 256:512], ab_a.key)
                    v(lambda e, o2=yv, i=dnt, a=av: e.tensor_tensor(
                          o2.ap, i.ap, a.ap, ALU.mult),
                      reads=[dnt.key, "psbig.3"], writes=[ab_a.key])
                    v(lambda e, o2=yv, bb=bv: e.tensor_tensor(
                          o2.ap, o2.ap, bb.ap, ALU.add),
                      reads=[ab_a.key, "psbig.3"], writes=[ab_a.key])
                    x1t = Tile(ab_b.ap[:, 0:BSL], ab_b.key)
                    dma_raw("sync", x1t.ap,
                            s_x1[b, o * 128:(o + 1) * 128, si * BSL:(si + 1) * BSL],
                            reads=["s_x1"], writes=[ab_b.key])
                    v(lambda e, o2=yv, i=x1t: e.tensor_tensor(
                          o2.ap, o2.ap, i.ap, ALU.add),
                      reads=[ab_a.key, ab_b.key], writes=[ab_a.key])
                    dma_raw("gpsimd",
                            yout[b, o * 128:(o + 1) * 128, si * BSL:(si + 1) * BSL],
                            yv.ap, reads=[ab_a.key], writes=["yout"])

            NSLB = NT // BSL
            for b in range(NB):
                if b == 0:
                    for h in range(HC):
                        b5pre_tile(h, 0)
                for si in range(NSLB):
                    b5_slice(b, si)
                    if b + 1 < NB:
                        for jj in range(3):
                            b5pre_tile(3 * si + jj, b + 1)

            dump_reg = {
                "s_x1": (s_x1, [NB, C, NT], F32), "s_ln3": (s_ln3, [NB, C, NT], F32),
                "s_q": (s_q, [NB, C, NT], F32), "s_k": (s_k, [NB, C, NT], F32),
                "s_vt": (s_vt, [NB, C, NT], F32),
                "s_gate": (s_gate, [3 * C, NB * NT], F32),
                "s_up": (s_up, [3 * C, NB * NT], F32),
                "s_g2r": (s_g2r, [3 * C, NB * NT], F32),
                "s_or": (s_or, [3 * C, NB * NT], F32),
            }
            for nm in DEBUG_DUMPS:
                if nm == "scal":
                    dap = nc.dram_tensor("d_scal", [128, 16], F32,
                                         kind="ExternalOutput").ap()
                    dma_raw("gpsimd", dap, scal.ap,
                            reads=[f"sc{i}" for i in range(16)], writes=[])
                    continue
                src_ap, shp, dt = dump_reg[nm]
                dap = nc.dram_tensor("d_" + nm, shp, dt, kind="ExternalOutput").ap()
                key = {"s_q": "s_qkv", "s_k": "s_qkv", "s_vt": "s_qkv"}.get(nm, nm)
                dma_raw("gpsimd", dap, src_ap, reads=[key], writes=[])

            sch.emit(nc, block, sems)
    return nc
def kernel(L0_attn_x, L0_attn_w_qkv, L0_attn_w_proj,
           L0_ffn_w_gate, L0_ffn_w_up, L0_ffn_w_down):
    x = np.asarray(L0_attn_x)
    eye = np.eye(128, dtype=np.float32)
    bones = np.zeros((128, 128), np.float32)
    bones[0:64, 0:64] = 1.0
    bones[64:128, 64:128] = 1.0
    import ml_dtypes
    onesb = np.ones((128, 128), ml_dtypes.bfloat16)

    common = {
        "wqkv": np.ascontiguousarray(L0_attn_w_qkv, np.float32),
        "wproj": np.ascontiguousarray(L0_attn_w_proj, np.float32),
        "wg": np.ascontiguousarray(np.asarray(L0_ffn_w_gate, np.float32)),
        "wu": np.ascontiguousarray(np.asarray(L0_ffn_w_up, np.float32)),
        "wd": np.ascontiguousarray(np.asarray(L0_ffn_w_down, np.float32)),
        "eye": eye, "d7": 7.0 * eye, "d15": 15.0 * eye, "d325": 3.25 * eye,
        "bones": bones, "onesb": onesb,
    }
    in_maps = []
    for c in range(8):
        xc = np.ascontiguousarray(
            x[c * NB:(c + 1) * NB].transpose(0, 2, 1).astype(np.float32))
        m = dict(common)
        m["x0"] = xc
        in_maps.append(m)

    nc = build_program()
    res = run_bass_kernel_spmd(nc, in_maps, core_ids=list(range(8)))
    outs = []
    for c in range(8):
        yc = np.asarray(res.results[c]["y"])
        outs.append(yc.transpose(0, 2, 1))
    return np.ascontiguousarray(np.concatenate(outs, axis=0)).astype(np.float32)
